# version 01
# baseline (speedup 1.0000x reference)
import numpy as np
from contextlib import ExitStack

import concourse.bass as bass
import concourse.mybir as mybir
from concourse.bass_utils import run_bass_kernel_spmd

F32 = mybir.dt.float32
BF16 = mybir.dt.bfloat16
I32 = mybir.dt.int32
AF = mybir.ActivationFunctionType
ALU = mybir.AluOpType
AX = mybir.AxisListType

DBG_PHASE = 4
DBG_SCAN = 4
DBG_GROUPS = None

T = 4096
NT = 32
NG = 8
D = 1024
DC = 8
LN_EPS = 1e-5
PHI_EPS = 1e-6

ENGS = ("sync", "tensor", "vector", "scalar", "gpsimd")


class Sem:
    __slots__ = ("h", "val", "name")


class Env:
    def __init__(self, nc, ctx):
        self.nc = nc
        self.ctx = ctx
        self.q = {k: [] for k in ENGS}
        self._sems = {}

    def sem(self, name):
        s = self._sems.get(name)
        if s is None:
            s = Sem()
            s.h = self.ctx.enter_context(self.nc.semaphore(name))
            s.val = 0
            s.name = name
            self._sems[name] = s
        return s

    def sb(self, ctx, name, shape, dtype):
        return ctx.enter_context(self.nc.sbuf_tensor(name, shape, dtype))

    def ps(self, ctx, name, shape, dtype=F32):
        return ctx.enter_context(self.nc.psum_tensor(name, shape, dtype))

    def emit(self, eng, fn, wait=(), inc=None, n=1, tag=""):
        wl = tuple(wait)
        if inc is not None:
            inc.val += n
            tgt = (inc, n)
        else:
            tgt = None

        def thunk(e, fn=fn, wl=wl, tgt=tgt):
            for sm, v in wl:
                e.wait_ge(sm.h, v)
            inst = fn(e)
            if tgt is not None:
                inst.then_inc(tgt[0].h, tgt[1])

        thunk.meta = (eng, tuple((sm.name, v) for sm, v in wl),
                      (inc.name, n) if inc is not None else None, tag)
        self.q[eng].append(thunk)
        return inc.val if inc is not None else None

    def check_deadlock(self):
        pos = {e: 0 for e in ENGS}
        vals = {}
        import collections
        total = sum(len(q) for q in self.q.values())
        done = 0
        while done < total:
            progressed = False
            for e in ENGS:
                while pos[e] < len(self.q[e]):
                    eng, wl, inc, tag = self.q[e][pos[e]].meta
                    if any(vals.get(sn, 0) < v for sn, v in wl):
                        break
                    if inc is not None:
                        vals[inc[0]] = vals.get(inc[0], 0) + inc[1]
                    pos[e] += 1
                    done += 1
                    progressed = True
            if not progressed:
                print("DEADLOCK: engine positions:", pos)
                for e in ENGS:
                    if pos[e] < len(self.q[e]):
                        eng, wl, inc, tag = self.q[e][pos[e]].meta
                        blocked = [(sn, v, vals.get(sn, 0)) for sn, v in wl
                                   if vals.get(sn, 0) < v]
                        print(f"  {e}[{pos[e]}] tag={tag!r} blocked on "
                              f"{blocked} inc={inc}")
                return False
        return True

    def dma(self, eng, out, in_, wait=(), inc=None, **kw):
        return self.emit(eng, lambda e: e.dma_start(out=out, in_=in_, **kw),
                         wait, inc, 16)

    def barrier(self):
        bar = self.sem("bar")
        drains = tuple((s, s.val) for s in self._sems.values() if s is not bar)
        self.emit("sync", lambda e: e.nop(), wait=drains)
        target = bar.val + len(ENGS)
        for eng in ENGS:
            self.emit(eng, lambda e: e.nop(), inc=bar)
        for eng in ENGS:
            self.emit(eng, lambda e: e.nop(), wait=((bar, target),))

    def replay(self):
        block = self.ctx.enter_context(self.nc.Block())
        for eng in ENGS:
            thunks = self.q[eng]
            getattr(block, eng)(lambda e, thunks=thunks: [t(e) for t in thunks])


def tile(dram_ap, i, p=128):
    return dram_ap.rearrange("(n p) d -> n p d", p=p)[i]



def emit_consts(E, ctx):
    C = {}
    C["iota_r"] = E.sb(ctx, "iota_r", [128, 128], F32)
    C["iota_c"] = E.sb(ctx, "iota_c", [128, 1], F32)
    C["ident"] = E.sb(ctx, "ident_bf", [128, 128], BF16)
    C["tri"] = E.sb(ctx, "tri_bf", [128, 128], BF16)
    C["ones_f"] = E.sb(ctx, "ones_f", [1, 128], F32)
    sg = E.sem("sgp")
    sd = E.sem("scst")
    v = E.emit("gpsimd", lambda e: e.iota(
        C["iota_r"][:], pattern=[[1, 128]], base=0, channel_multiplier=0,
        allow_small_or_imprecise_dtypes=True), inc=sg)
    v = E.emit("gpsimd", lambda e: e.iota(
        C["iota_c"][:], pattern=[[1, 1]], base=0, channel_multiplier=1,
        allow_small_or_imprecise_dtypes=True), inc=sg)
    E.emit("vector", lambda e: e.tensor_scalar(
        C["ident"][:], C["iota_r"][:], C["iota_c"][:], None, ALU.is_equal),
        wait=((sg, v),), inc=sd)
    E.emit("vector", lambda e: e.tensor_scalar(
        C["tri"][:], C["iota_r"][:], C["iota_c"][:], None, ALU.is_ge),
        wait=((sg, v),), inc=sd)
    E.emit("vector", lambda e: e.memset(C["ones_f"][:], 1.0), inc=sd)
    cvals = {"ln_eps": LN_EPS, "phi_eps": PHI_EPS, "l2_eps": 1e-12,
             "neg8": -8.0, "dq": 1.0 / 127.0, "one": 1.0}
    cv = E.sb(ctx, "cvals", [128, len(cvals)], F32)
    for i, (k, v) in enumerate(cvals.items()):
        E.emit("vector", lambda e, i=i, v=v: e.memset(cv[:, i:i + 1], v),
               inc=sd)
        C[k] = cv[:, i:i + 1]
    C["ready"] = (sd, sd.val)
    return C



def emit_ln_tile(E, C, st, col, x_ap, xn_out=None, wait_x=()):
    sa, sd = E.sem("lna"), E.sem("lnd")
    cs = slice(col, col + 1)
    prev = st.setdefault("_d1", {})
    wx = tuple(wait_x) + (((sd, prev[col]),) if prev.get(col) else ())
    a1 = E.emit("scalar", lambda e: e.activation(
        st["sq"][:], x_ap, AF.Square, accum_out=st["ssq"][:, cs]),
        wait=wx, inc=sa)
    v = E.emit("vector", lambda e: e.reduce_sum(
        st["sum"][:, cs], x_ap, axis=AX.X), wait=wait_x, inc=sd)
    v = E.emit("vector", lambda e: e.tensor_scalar_mul(
        st["mu"][:, cs], st["sum"][:, cs], 1.0 / D), wait=((sd, v),), inc=sd)
    v = E.emit("vector", lambda e: e.tensor_mul(
        st["mu2"][:, cs], st["mu"][:, cs], st["mu"][:, cs]),
        wait=((sd, v),), inc=sd)
    d1 = E.emit("vector", lambda e: e.scalar_tensor_tensor(
        st["var"][:, cs], st["ssq"][:, cs], 1.0 / D, st["mu2"][:, cs],
        ALU.mult, ALU.subtract), wait=((sa, a1), (sd, v)), inc=sd)
    prev[col] = d1
    E.emit("scalar", lambda e: e.activation(
        st["lnv"][:, cs], st["var"][:, cs], AF.Ln, bias=C["ln_eps"]),
        wait=((sd, d1), C["ready"]))
    a2 = E.emit("scalar", lambda e: e.activation(
        st["rstd"][:, cs], st["lnv"][:, cs], AF.Exp, scale=-0.5), inc=sa)
    d2 = None
    if xn_out is not None:
        d2 = E.emit("vector", lambda e: e.tensor_scalar(
            xn_out, x_ap, st["mu"][:, cs], st["rstd"][:, cs],
            ALU.subtract, ALU.mult), wait=((sa, a2), (sd, d1)), inc=sd)
    return a2, d2


def mk_ln_stats(E, ctx, name, ncols):
    st = {}
    for k in ("sum", "ssq", "mu", "mu2", "var", "lnv", "rstd"):
        st[k] = E.sb(ctx, f"{name}_{k}", [128, ncols], F32)
    st["sq"] = E.sb(ctx, f"{name}_sq", [128, D], F32)
    return st



def emit_l0_attn(E, C, x_src, x_dst, W):
    nc = E.nc
    with ExitStack() as c:
        wq = E.sb(c, "a0_wq", [128, DC * 1024], BF16)
        wk = E.sb(c, "a0_wk", [128, DC * 1024], BF16)
        wv = E.sb(c, "a0_wv", [128, DC * 1024], BF16)
        wo = E.sb(c, "a0_wo", [128, DC * 1024], BF16)
        xg = E.sb(c, "a0_xg", [128, 4 * 1024], F32)
        xng = E.sb(c, "a0_xng", [128, 4 * 1024], BF16)
        xnt = E.sb(c, "a0_xnt", [128, DC * 512], BF16)
        qft = E.sb(c, "a0_qft", [128, DC * 512], BF16)
        kft = E.sb(c, "a0_kft", [128, DC * 512], BF16)
        kfg = E.sb(c, "a0_kfg", [128, 4 * 1024], BF16)
        vag = E.sb(c, "a0_vag", [128, 4 * 16 * 65], BF16)
        oag = E.sb(c, "a0_oag", [128, 4 * 16 * 65], BF16)
        oal = E.sb(c, "a0_oal", [128, 4 * 1024], BF16)
        oat = E.sb(c, "a0_oat", [128, DC * 512], BF16)
        kvf = E.sb(c, "a0_kvf", [128, 8 * 65], F32)
        kvb = E.sb(c, "a0_kvb", [128, 8 * 65], BF16)
        tmpa = E.sb(c, "a0_tmpa", [128, 512], F32)
        tmpb = E.sb(c, "a0_tmpb", [128, 512], F32)
        s_sb = E.sb(c, "a0_ssb", [128, 2 * 512], BF16)
        den = E.sb(c, "a0_den", [128, 16], F32)
        denr = E.sb(c, "a0_denr", [128, 16], F32)
        outf = E.sb(c, "a0_outf", [128, 2 * 1024], F32)
        tmpr = E.sb(c, "a0_tmpr", [128, 2 * 1024], F32)
        st = mk_ln_stats(E, c, "a0_st", 8)

        ptr = [E.ps(c, "a0_ptr0", [128, 1024], BF16),
               E.ps(c, "a0_ptr1", [128, 1024], BF16)]
        ppj = [E.ps(c, "a0_ppj0", [128, 512], F32),
               E.ps(c, "a0_ppj1", [128, 512], F32)]
        po = [E.ps(c, f"a0_po{i}", [128, 65], F32) for i in range(4)]

        sw = E.sem("sw")
        sx = E.sem("sx")
        sst = E.sem("sst")
        smm = E.sem("smm")
        sev = E.sem("sev")
        sac = E.sem("sac")
        lnd = E.sem("lnd")

        for wsb, key in ((wq, "l0_wq"), (wk, "l0_wk"), (wv, "l0_wv"),
                         (wo, "l0_wo")):
            for ci in range(DC):
                E.dma("gpsimd", wsb[:, ci * 1024:(ci + 1) * 1024],
                      W[key][ci * 128:(ci + 1) * 128, :], inc=sw)
        w_ready = sw.val

        E.emit("vector", lambda e: e.memset(kvf[:], 0.0))
        E.emit("vector", lambda e: e.memset(kvb[:], 0.0), inc=sev)
        kv_ready = sev.val
        vag4 = vag[:].rearrange("p (t h v) -> p t h v", t=4, h=16)
        E.emit("vector", lambda e: e.memset(vag4[:, :, :, 64:65], 1.0),
               inc=sev)

        cst = E.sem("scst")
        cst_ready = cst.val

        tr_ev = [0, 0]
        pj_free = [0, 0]
        pj_free_sem = [None, None]
        po_free = [0, 0, 0, 0]
        s_used = [0, 0]
        kvb_ready = kv_ready
        xg_free = 0
        phi_prev = 0
        ln_prev = [0, 0]

        ngroups = DBG_GROUPS or NG
        for g in range(ngroups):
            for tt in range(4):
                src = tile(x_src, 4 * g + tt)
                E.dma("sync", xg[:, tt * 1024:(tt + 1) * 1024], src,
                      wait=(((sst, xg_free),) if g > 0 else ()), inc=sx)
            x_ready = sx.val

            ln_done = []
            for tt in range(4):
                xa = xg[:, tt * 1024:(tt + 1) * 1024]
                na = xng[:, tt * 1024:(tt + 1) * 1024]
                _, d2 = emit_ln_tile(E, C, st, tt, xa, xn_out=na,
                                     wait_x=((sx, x_ready),))
                ln_done.append(d2)

            for ci in range(DC):
                pt = ptr[ci % 2]
                for tt in range(4):
                    src = xng[:, tt * 1024 + ci * 128: tt * 1024 + (ci + 1) * 128]
                    dst = pt[:, tt * 128:(tt + 1) * 128]
                    wl = [(lnd, ln_done[tt])]
                    if g == 0 and ci < 2 and tt == 0:
                        wl.append(C["ready"])
                    if tr_ev[ci % 2]:
                        wl.append((sev, tr_ev[ci % 2]))
                    E.emit("tensor", lambda e, d=dst, s=src: e.transpose(
                        d, s, C["ident"][:]), wait=wl,
                        inc=(smm if tt == 3 else None))
                mv = smm.val
                tr_ev[ci % 2] = E.emit(
                    "vector", lambda e, pt=pt, ci=ci: e.tensor_copy(
                        xnt[:, ci * 512:(ci + 1) * 512], pt[:, 0:512]),
                    wait=((smm, mv),), inc=sev)
            xnt_ready = sev.val

            for which, wsb, dstt in ((("q", wq, qft), ("k", wk, kft)) if DBG_PHASE >= 2 else ()):
                for oc in range(DC):
                    p = ppj[oc % 2]
                    for ci in range(DC):
                        lhs = wsb[:, ci * 1024 + oc * 128: ci * 1024 + (oc + 1) * 128]
                        rhs = xnt[:, ci * 512:(ci + 1) * 512]
                        wl = []
                        if ci == 0:
                            wl = [(sev, xnt_ready), (sw, w_ready)]
                            if pj_free_sem[oc % 2] is not None:
                                wl.append((pj_free_sem[oc % 2], pj_free[oc % 2]))
                        E.emit("tensor", lambda e, p=p, l=lhs, r=rhs, ci=ci:
                               e.matmul(p[:], l, r, start=(ci == 0),
                                        stop=(ci == DC - 1)), wait=wl,
                               inc=(smm if ci == DC - 1 else None))
                    mv = smm.val
                    a1 = E.emit("scalar", lambda e, p=p: e.activation(
                        tmpa[:], p[:], AF.Abs),
                        wait=((smm, mv), (sev, phi_prev)), inc=sac)
                    e1 = E.emit("vector", lambda e, p=p: e.tensor_scalar_max(
                        tmpb[:], p[:], 0.0), wait=((sac, a1),), inc=sev)
                    E.emit("scalar", lambda e: e.activation(
                        tmpa[:], tmpa[:], AF.Exp, scale=-1.0))
                    a2 = E.emit("scalar", lambda e: e.activation(
                        tmpa[:], tmpa[:], AF.Ln, bias=1.0), inc=sac)
                    dst = dstt[:, oc * 512:(oc + 1) * 512]
                    e2 = E.emit("vector", lambda e, d=dst: e.scalar_tensor_tensor(
                        d, tmpa[:], PHI_EPS, tmpb[:], ALU.add, ALU.add),
                        wait=((sac, a2), (sev, e1)), inc=sev)
                    phi_prev = e2
                    pj_free[oc % 2] = e1
                    pj_free_sem[oc % 2] = sev
            qk_ready = sev.val

            for tt in (range(4) if DBG_PHASE >= 2 else ()):
                for oc in range(2):
                    p = ppj[(tt * 2 + oc) % 2]
                    for ci in range(DC):
                        lhs = xnt[:, ci * 512 + tt * 128: ci * 512 + (tt + 1) * 128]
                        rhs = wv[:, ci * 1024 + oc * 512: ci * 1024 + (oc + 1) * 512]
                        wl = []
                        if ci == 0:
                            wl = [(pj_free_sem[(tt * 2 + oc) % 2],
                                   pj_free[(tt * 2 + oc) % 2])]
                        E.emit("tensor", lambda e, p=p, l=lhs, r=rhs, ci=ci:
                               e.matmul(p[:], l, r, start=(ci == 0),
                                        stop=(ci == DC - 1)), wait=wl,
                               inc=(smm if ci == DC - 1 else None))
                    mv = smm.val
                    dst = vag4[:, tt, oc * 8:(oc + 1) * 8, 0:64]
                    psrc = p[:].rearrange("p (h v) -> p h v", h=8)
                    ev = E.emit("vector", lambda e, d=dst, s=psrc: e.tensor_copy(
                        d, s), wait=((smm, mv),), inc=sev)
                    pj_free[(tt * 2 + oc) % 2] = ev
                    pj_free_sem[(tt * 2 + oc) % 2] = sev
            v_ready = sev.val

            kfg3 = kfg[:].rearrange("p (t d) -> p t d", t=4)
            for ci in (range(DC) if DBG_PHASE >= 3 else ()):
                pt = ptr[ci % 2]
                for tt in range(4):
                    src = kft[:, ci * 512 + tt * 128: ci * 512 + (tt + 1) * 128]
                    dst = pt[:, tt * 128:(tt + 1) * 128]
                    wl = [(sev, qk_ready)] if tt == 0 else []
                    if tt == 0 and tr_ev[ci % 2]:
                        wl.append((sev, tr_ev[ci % 2]))
                    E.emit("tensor", lambda e, d=dst, s=src: e.transpose(
                        d, s, C["ident"][:]), wait=wl,
                        inc=(smm if tt == 3 else None))
                mv = smm.val
                for tt in range(4):
                    dst = kfg[:, tt * 1024 + ci * 128: tt * 1024 + (ci + 1) * 128]
                    src = pt[:, tt * 128:(tt + 1) * 128]
                    tr_ev[ci % 2] = E.emit(
                        "vector", lambda e, d=dst, s=src: e.tensor_copy(d, s),
                        wait=(((smm, mv),) if tt == 0 else ()), inc=sev)
            kf_ready = sev.val

            oag4 = oag[:].rearrange("p (t h v) -> p t h v", t=4, h=16)
            for hp in (range(8) if DBG_PHASE >= 3 else ()):
                for sub in range(2):
                    h = 2 * hp + sub
                    rb = 64 * sub
                    cc = hp
                    for tt in (range(4) if DBG_SCAN >= 2 else ()):
                        lhs = qft[rb:rb + 64,
                                  cc * 512 + tt * 128: cc * 512 + (tt + 1) * 128]
                        rhs = kvb[rb:rb + 64, cc * 65:(cc + 1) * 65]
                        wl = [(sev, max(qk_ready, kvb_ready, po_free[tt]))]
                        E.emit("tensor", lambda e, p=po[tt], l=lhs, r=rhs:
                               e.matmul(p[:], l, r, start=True, stop=False),
                               wait=wl)
                    for kt in (range(4) if DBG_SCAN >= 1 else ()):
                        ps = ppj[kt % 2]
                        n = (4 - kt) * 128
                        lhs = kft[rb:rb + 64,
                                  cc * 512 + kt * 128: cc * 512 + (kt + 1) * 128]
                        rhs = qft[rb:rb + 64, cc * 512 + kt * 128: (cc + 1) * 512]
                        wl = [(pj_free_sem[kt % 2], pj_free[kt % 2])] \
                            if pj_free_sem[kt % 2] is not None else []
                        mv = E.emit("tensor", lambda e, p=ps, l=lhs, r=rhs, kt=kt:
                                    e.matmul(p[:, kt * 128:512], l, r,
                                             start=True, stop=True),
                                    wait=wl, inc=smm)
                        slot = kt % 2
                        wl = [(smm, max(mv, s_used[slot]))]
                        dd = s_sb[:, slot * 512 + kt * 128: slot * 512 + (kt + 1) * 128]
                        sschunk = ps[:, kt * 128:(kt + 1) * 128]
                        mk = E.emit("vector", lambda e, d=dd, s=sschunk: e.tensor_tensor(
                            d, s, C["tri"][:], ALU.mult), wait=wl, inc=sev)
                        if kt < 3:
                            dd2 = s_sb[:, slot * 512 + (kt + 1) * 128: slot * 512 + 512]
                            ss2 = ps[:, (kt + 1) * 128: 512]
                            ev = E.emit("vector", lambda e, d=dd2, s=ss2:
                                        e.tensor_copy(d, s), inc=sev)
                        else:
                            ev = mk
                        pj_free[kt % 2] = ev
                        pj_free_sem[kt % 2] = sev
                        s_ready = ev
                        for tt in (range(kt, 4) if DBG_SCAN >= 3 else ()):
                            lhs = s_sb[:, slot * 512 + tt * 128: slot * 512 + (tt + 1) * 128]
                            rhs = vag4[:, kt, h, :]
                            wl = [(sev, max(s_ready, v_ready))]
                            do_inc = (kt == tt) or (tt == 3)
                            E.emit("tensor", lambda e, p=po[tt], l=lhs, r=rhs,
                                   last=(kt == tt): e.matmul(
                                       p[:], l, r, start=False, stop=last),
                                   wait=wl,
                                   inc=(smm if do_inc else None))
                            if kt == tt:
                                po_mm = smm.val
                                dst = oag4[:, tt, h, :]
                                po_free[tt] = E.emit(
                                    "vector", lambda e, d=dst, p=po[tt]:
                                    e.tensor_copy(d, p[:]),
                                    wait=((smm, po_mm),), inc=sev)
                        s_used[slot] = smm.val
                wl = [(sev, max(po_free[0], kf_ready, v_ready))]
                first = True
                for sub in (range(2) if DBG_SCAN >= 4 else ()):
                    h = 2 * hp + sub
                    for kt in range(4):
                        lhs = kfg3[:, kt, h * 64:(h + 1) * 64]
                        rhs = vag4[:, kt, h, :]
                        outp = po[0][0:64, :] if sub == 0 else po[0][64:128, :]
                        tp = None if sub == 0 else (0, 64)
                        E.emit("tensor", lambda e, o=outp, l=lhs, r=rhs,
                               st_=(kt == 0), sp=(kt == 3), tp=tp:
                               e.matmul(o, l, r, start=st_, stop=sp,
                                        tile_position=tp),
                               wait=(wl if first else ()),
                               inc=(smm if (sub == 1 and kt == 3) else None))
                        first = False
                if DBG_SCAN < 4:
                    continue
                kv_mm = smm.val
                v = E.emit("vector", lambda e, cc=hp: e.tensor_tensor(
                    kvf[:, cc * 65:(cc + 1) * 65], kvf[:, cc * 65:(cc + 1) * 65],
                    po[0][:], ALU.add), wait=((smm, kv_mm),), inc=sev)
                po_free[0] = E.emit("vector", lambda e, cc=hp: e.tensor_copy(
                    kvb[:, cc * 65:(cc + 1) * 65], kvf[:, cc * 65:(cc + 1) * 65]),
                    wait=((sev, v),), inc=sev)
            kvb_ready = sev.val

            oal4 = oal[:].rearrange("p (t d) -> p t d", t=4)
            div_done = []
            for tt in (range(4) if DBG_PHASE >= 4 else ()):
                wl = [(sev, kvb_ready)]
                v = E.emit("vector", lambda e, tt=tt: e.tensor_scalar(
                    den[:], oag4[:, tt, :, 64], PHI_EPS, None, ALU.add),
                    wait=wl, inc=sev)
                v = E.emit("vector", lambda e: e.reciprocal(denr[:], den[:]),
                           wait=((sev, v),), inc=sev)
                dv = E.emit("vector", lambda e, tt=tt: e.tensor_tensor(
                    oal4[:, tt, :].rearrange("p (h v) -> p h v", h=16),
                    oag4[:, tt, :, 0:64],
                    denr[:].rearrange("p (h o) -> p h o", o=1).broadcast_to(
                        (128, 16, 64)),
                    ALU.mult), wait=((sev, v),), inc=sev)
                div_done.append(dv)

            for ci in (range(DC) if DBG_PHASE >= 4 else ()):
                pt = ptr[ci % 2]
                for tt in range(4):
                    src = oal[:, tt * 1024 + ci * 128: tt * 1024 + (ci + 1) * 128]
                    dst = pt[:, tt * 128:(tt + 1) * 128]
                    wl = [(sev, div_done[tt])]
                    if tt == 0 and tr_ev[ci % 2]:
                        wl.append((sev, tr_ev[ci % 2]))
                    E.emit("tensor", lambda e, d=dst, s=src: e.transpose(
                        d, s, C["ident"][:]), wait=wl,
                        inc=(smm if tt == 3 else None))
                mv = smm.val
                tr_ev[ci % 2] = E.emit(
                    "vector", lambda e, pt=pt, ci=ci: e.tensor_copy(
                        oat[:, ci * 512:(ci + 1) * 512], pt[:, 0:512]),
                    wait=((smm, mv),), inc=sev)
            oat_ready = sev.val

            if DBG_PHASE < 4:
                import os as _os
                which = _os.environ.get("DBG_OUT", "x")
                for tt in range(4):
                    xa = xg[:, tt * 1024:(tt + 1) * 1024]
                    src = {"x": xa,
                           "xn": xng[:, tt * 1024:(tt + 1) * 1024],
                           "kf": kfg[:, tt * 1024:(tt + 1) * 1024],
                           "oal": oal[:, tt * 1024:(tt + 1) * 1024],
                           }.get(which)
                    buf = tmpr if tt % 2 == 0 else outf
                    dd = buf[:, (tt // 2) * 1024:(tt // 2 + 1) * 1024]
                    if which == "stats":
                        wl = ((sev, sev.val), (lnd, E.sem("lnd").val),
                              (E.sem("lna"), E.sem("lna").val), (sst, sst.val))
                        for j, key in enumerate(("sum", "ssq", "mu", "mu2",
                                                 "var", "lnv", "rstd")):
                            cv = E.emit("vector", lambda e, d=dd, j=j, k=key, tt=tt:
                                        e.tensor_copy(d[:, j:j+1],
                                                      st[k][:, tt:tt+1]),
                                        wait=(wl if j == 0 else ()), inc=sev)
                    else:
                        cv = E.emit("vector", lambda e, d=dd, s=src: e.tensor_copy(
                            d, s), wait=((sev, sev.val), (lnd, E.sem("lnd").val),
                                         (sst, sst.val)),
                            inc=sev)
                    E.dma("sync", tile(x_dst, 4 * g + tt), dd,
                          wait=((sev, cv),), inc=sst)
                xg_free = sst.val
                continue
            for tt in range(4):
                of = outf[:, (tt % 2) * 1024:(tt % 2) * 1024 + 1024]
                ofr = tmpr[:, (tt % 2) * 1024:(tt % 2) * 1024 + 1024]
                ev_out = []
                for oc in range(2):
                    p = ppj[oc % 2]
                    for ci in range(DC):
                        lhs = oat[:, ci * 512 + tt * 128: ci * 512 + (tt + 1) * 128]
                        rhs = wo[:, ci * 1024 + oc * 512: ci * 1024 + (oc + 1) * 512]
                        wl = []
                        if ci == 0:
                            wl = [(sev, oat_ready),
                                  (pj_free_sem[oc % 2], pj_free[oc % 2])]
                        E.emit("tensor", lambda e, p=p, l=lhs, r=rhs, ci=ci:
                               e.matmul(p[:], l, r, start=(ci == 0),
                                        stop=(ci == DC - 1)), wait=wl,
                               inc=(smm if ci == DC - 1 else None))
                    mv = smm.val
                    wlv = [(smm, mv)]
                    if ln_prev[tt % 2]:
                        wlv.append((E.sem("lna"), ln_prev[tt % 2]))
                    ev = E.emit("vector", lambda e, d=of, oc=oc, p=p:
                                e.tensor_copy(d[:, oc * 512:(oc + 1) * 512], p[:]),
                                wait=wlv, inc=sev)
                    ev_out.append(ev)
                    pj_free[oc % 2] = ev
                    pj_free_sem[oc % 2] = sev
                a2, _ = emit_ln_tile(E, C, st, 4 + tt % 2, of,
                                     wait_x=((sev, ev_out[-1]),))
                ln_prev[tt % 2] = a2
                lna = E.sem("lna")
                cs = slice(4 + tt % 2, 5 + tt % 2)
                v = E.emit("vector", lambda e, d=ofr, s=of, cs=cs: e.tensor_scalar(
                    d, s, st["mu"][:, cs], st["rstd"][:, cs],
                    ALU.subtract, ALU.mult),
                    wait=((lna, a2), (lnd, E.sem("lnd").val)), inc=sev)
                xa = xg[:, tt * 1024:(tt + 1) * 1024]
                rv = E.emit("vector", lambda e, xa=xa, d=ofr: e.tensor_add(
                    xa, xa, d), wait=((sev, v),), inc=sev)
                E.dma("sync", tile(x_dst, 4 * g + tt), xa,
                      wait=((sev, rv),), inc=sst)
            xg_free = sst.val



def emit_l0_moe(E, C, x_src, x_dst, W):
    NE = 10
    HC = 16
    with ExitStack() as c:
        w1 = E.sb(c, "m_w1", [128, DC * 2048], BF16)
        w3 = E.sb(c, "m_w3", [128, DC * 2048], BF16)
        w2 = E.sb(c, "m_w2", [128, HC * 1024], BF16)
        router = E.sb(c, "m_rt", [128, DC * 8], BF16)
        bias_bc = E.sb(c, "m_bias", [128, 8], F32)
        rb_sb = E.sb(c, "m_rb", [1, 8], F32)
        cw = E.sb(c, "m_cw", [128, NT * 8], F32)
        xg = E.sb(c, "m_xg", [128, 4 * 1024], F32)
        xng = E.sb(c, "m_xng", [128, 4 * 1024], BF16)
        xnt = E.sb(c, "m_xnt", [128, DC * 512], BF16)
        xbuf = [E.sb(c, "m_xb0", [128, DC * 512], BF16),
                E.sb(c, "m_xb1", [128, DC * 512], BF16)]
        hfm = E.sb(c, "m_hfm", [128, HC * 512], BF16)
        tmps = E.sb(c, "m_tmps", [128, 512], F32)
        contrib = E.sb(c, "m_ctb", [128, 4 * 512], F32)
        lg = E.sb(c, "m_lg", [128, 8], F32)
        lg2 = E.sb(c, "m_lg2", [128, 8], F32)
        mm1 = E.sb(c, "m_m1", [128, 4], F32)
        msk = E.sb(c, "m_msk", [128, 8], F32)
        msk2 = E.sb(c, "m_msk2", [128, 8], F32)
        acc_sb = E.sb(c, "m_acc", [128, 2 * 1024], F32)
        tmpr = E.sb(c, "m_tmpr", [128, 2 * 1024], F32)
        st = mk_ln_stats(E, c, "m_st", 8)

        sw = E.sem("sw")
        sx = E.sem("sx")
        sst = E.sem("sst")
        smm = E.sem("smm")
        sev = E.sem("sev")
        sac = E.sem("sac")
        sxt = [E.sem("sxt0"), E.sem("sxt1")]
        ssp = E.sem("ssp")
        sad = E.sem("sad")
        lna, lnd = E.sem("lna"), E.sem("lnd")

        acc = W["moe_acc"]
        spill = W["xnt_spill"]
        spill_r = spill.rearrange("(c p) t -> p c t", p=128)

        def load_expert_w13(e, wait):
            for ci in range(DC):
                E.dma("gpsimd", w1[:, ci * 2048:(ci + 1) * 2048],
                      W["moe_w1t"][e, ci * 128:(ci + 1) * 128, :],
                      wait=wait, inc=sw)
                E.dma("gpsimd", w3[:, ci * 2048:(ci + 1) * 2048],
                      W["moe_w3t"][e, ci * 128:(ci + 1) * 128, :],
                      wait=wait, inc=sw)
            return sw.val

        def load_expert_w2(e, wait):
            for hc in range(HC):
                E.dma("gpsimd", w2[:, hc * 1024:(hc + 1) * 1024],
                      W["moe_w2t"][e, hc * 128:(hc + 1) * 128, :],
                      wait=wait, inc=sw)
            return sw.val

        for ci in range(DC):
            E.dma("gpsimd", router[:, ci * 8:(ci + 1) * 8],
                  W["router_t"][ci * 128:(ci + 1) * 128, :], inc=sw)
        srb = E.sem("srb")
        E.dma("sync", rb_sb[:], W["router_bias"], inc=srb)
        rt_ready = sw.val
        rb_ready = srb.val
        w13_ready = load_expert_w13(0, ())
        w2_ready = load_expert_w2(0, ())

        with ExitStack() as cp:
            ptr = [E.ps(cp, "m_ptr0", [128, 1024], BF16),
                   E.ps(cp, "m_ptr1", [128, 1024], BF16)]
            pr = E.ps(cp, "m_pr", [128, 8], F32)
            pb = E.ps(cp, "m_pb", [128, 8], F32)

            bmm = E.emit("tensor", lambda e: e.matmul(
                pb[:], C["ones_f"][:], rb_sb[:], start=True, stop=True),
                wait=((srb, rb_ready), C["ready"]), inc=smm)
            bev = E.emit("vector", lambda e: e.tensor_copy(
                bias_bc[:], pb[:]), wait=((smm, bmm),), inc=sev)

            tr_ev = [0, 0]
            pr_free = (None, 0)
            ln_gate = 0
            for g in range(NG):
                for tt in range(4):
                    E.dma("sync", xg[:, tt * 1024:(tt + 1) * 1024],
                          tile(x_src, 4 * g + tt),
                          wait=(((lnd, ln_gate),) if g > 0 else ()), inc=sx)
                x_ready = sx.val
                ln_done = []
                for tt in range(4):
                    xa = xg[:, tt * 1024:(tt + 1) * 1024]
                    na = xng[:, tt * 1024:(tt + 1) * 1024]
                    _, d2 = emit_ln_tile(E, C, st, tt, xa, xn_out=na,
                                         wait_x=((sx, x_ready),))
                    ln_done.append(d2)
                for ci in range(DC):
                    pt = ptr[ci % 2]
                    for tt in range(4):
                        src = xng[:, tt * 1024 + ci * 128:
                                  tt * 1024 + (ci + 1) * 128]
                        dst = pt[:, tt * 128:(tt + 1) * 128]
                        wl = [(lnd, ln_done[tt])]
                        if tr_ev[ci % 2]:
                            wl.append((sev, tr_ev[ci % 2]))
                        E.emit("tensor", lambda e, d=dst, s=src: e.transpose(
                            d, s, C["ident"][:]), wait=wl,
                            inc=(smm if tt == 3 else None))
                    mv = smm.val
                    tr_ev[ci % 2] = E.emit(
                        "vector", lambda e, pt=pt, ci=ci: e.tensor_copy(
                            xnt[:, ci * 512:(ci + 1) * 512], pt[:, 0:512]),
                        wait=((smm, mv),), inc=sev)
                xnt_ready = sev.val
                for ci in range(DC):
                    E.dma("sync", spill[ci * 128:(ci + 1) * 128,
                                        g * 512:(g + 1) * 512],
                          xnt[:, ci * 512:(ci + 1) * 512],
                          wait=((sev, xnt_ready),), inc=ssp)
                for tt in range(4):
                    gt = 4 * g + tt
                    for ci in range(DC):
                        lhs = xnt[:, ci * 512 + tt * 128: ci * 512 + (tt + 1) * 128]
                        rhs = router[:, ci * 8:(ci + 1) * 8]
                        wl = []
                        if ci == 0:
                            wl = [(sev, xnt_ready), (sw, rt_ready)]
                            if pr_free[0] is not None:
                                wl.append((pr_free[0], pr_free[1]))
                        E.emit("tensor", lambda e, l=lhs, r=rhs, ci=ci:
                               e.matmul(pr[:], l, r, start=(ci == 0),
                                        stop=(ci == DC - 1)), wait=wl,
                               inc=(smm if ci == DC - 1 else None))
                    mv = smm.val
                    v = E.emit("vector", lambda e: e.tensor_tensor(
                        lg[:], pr[:], bias_bc[:], ALU.add),
                        wait=((smm, mv), (sev, bev)), inc=sev)
                    pr_free = (sev, v)
                    v = E.emit("vector", lambda e: e.reduce_max(
                        mm1[:, 0:1], lg[:], axis=AX.X), wait=((sev, v),),
                        inc=sev)
                    v = E.emit("vector", lambda e: e.tensor_scalar(
                        msk[:], lg[:], mm1[:, 0:1], None, ALU.is_ge),
                        wait=((sev, v),), inc=sev)
                    v = E.emit("vector", lambda e: e.scalar_tensor_tensor(
                        lg2[:], msk[:], -1e30, lg[:], ALU.mult, ALU.add),
                        wait=((sev, v),), inc=sev)
                    v = E.emit("vector", lambda e: e.reduce_max(
                        mm1[:, 1:2], lg2[:], axis=AX.X), wait=((sev, v),),
                        inc=sev)
                    v = E.emit("vector", lambda e: e.tensor_scalar(
                        msk2[:], lg2[:], mm1[:, 1:2], None, ALU.is_ge),
                        wait=((sev, v),), inc=sev)
                    v = E.emit("vector", lambda e: e.tensor_sub(
                        mm1[:, 2:3], mm1[:, 0:1], mm1[:, 1:2]),
                        wait=((sev, v),), inc=sev)
                    a = E.emit("scalar", lambda e: e.activation(
                        mm1[:, 3:4], mm1[:, 2:3], AF.Sigmoid),
                        wait=((sev, v),), inc=sac)
                    v = E.emit("vector", lambda e: e.tensor_scalar(
                        mm1[:, 2:3], mm1[:, 3:4], -1.0, 1.0, ALU.mult,
                        ALU.add), wait=((sac, a),), inc=sev)
                    v = E.emit("vector", lambda e: e.tensor_scalar(
                        msk[:], msk[:], mm1[:, 3:4], None, ALU.mult),
                        wait=((sev, v),), inc=sev)
                    v = E.emit("vector", lambda e, gt=gt: e.scalar_tensor_tensor(
                        cw[:, gt * 8:(gt + 1) * 8], msk2[:], mm1[:, 2:3],
                        msk[:], ALU.mult, ALU.add), wait=((sev, v),), inc=sev)
                ln_gate = E.sem("lnd").val
            pre_mm = smm.val
            pre_ev = sev.val
            spill_done = ssp.val

        pg = [None, None]
        pu = [None, None]
        po = [None, None]
        with ExitStack() as ce:
            pg[0] = E.ps(ce, "m_pg0", [128, 512], F32)
            pg[1] = E.ps(ce, "m_pg1", [128, 512], F32)
            pu[0] = E.ps(ce, "m_pu0", [128, 512], F32)
            pu[1] = E.ps(ce, "m_pu1", [128, 512], F32)
            po[0] = E.ps(ce, "m_po0", [128, 512], F32)
            po[1] = E.ps(ce, "m_po1", [128, 512], F32)

            xb_free = [0, 0]
            pg_free = [(None, 0), (None, 0)]
            pu_free = [(None, 0), (None, 0)]
            po_free = [(None, 0), (None, 0)]
            ctb_free = {}
            w13_mm = 0
            w2_mm = 0
            silu_prev = 0
            pr_done = (smm, pre_mm)

            for e in range(NE):
                E.dma("sync", xbuf[0][:].rearrange("p (c t) -> p c t", c=DC),
                      spill_r[:, :, 0:512],
                      wait=((ssp, spill_done), (smm, xb_free[0])),
                      inc=sxt[0])
                x0_ready = sxt[0].val
                xrdy = [x0_ready, 0]
                for s in range(NG):
                    sl = s % 2
                    if s + 1 < NG:
                        nsl = (s + 1) % 2
                        E.dma("sync",
                              xbuf[nsl][:].rearrange("p (c t) -> p c t", c=DC),
                              spill_r[:, :, (s + 1) * 512:(s + 2) * 512],
                              wait=((smm, xb_free[nsl]),), inc=sxt[nsl])
                        xrdy[nsl] = sxt[nsl].val
                    for h in range(HC):
                        hp = h % 2
                        for which, wsb, pp, pfree in (
                                ("g", w1, pg[hp], pg_free[hp]),
                                ("u", w3, pu[hp], pu_free[hp])):
                            for ci in range(DC):
                                lhs = wsb[:, ci * 2048 + h * 128:
                                          ci * 2048 + (h + 1) * 128]
                                rhs = xbuf[sl][:, ci * 512:(ci + 1) * 512]
                                wl = []
                                if ci == 0:
                                    wl = [(sxt[sl], xrdy[sl]),
                                          (sw, w13_ready)]
                                    if e == 0 and s == 0 and h < 2:
                                        wl.append(pr_done)
                                        wl.append((sev, pre_ev))
                                    if pfree[0] is not None:
                                        wl.append(pfree)
                                E.emit("tensor", lambda e_, p=pp, l=lhs,
                                       r=rhs, ci=ci: e_.matmul(
                                           p[:], l, r, start=(ci == 0),
                                           stop=(ci == DC - 1)), wait=wl,
                                       inc=(smm if ci == DC - 1 else None))
                        mv = smm.val
                        a = E.emit("scalar", lambda e_, p=pg[hp]: e_.activation(
                            tmps[:], p[:], AF.Silu),
                            wait=((smm, mv), (sev, silu_prev)), inc=sac)
                        pg_free[hp] = (sac, a)
                        dv = E.emit("vector", lambda e_, p=pu[hp], h=h:
                                    e_.tensor_tensor(
                                        hfm[:, h * 512:(h + 1) * 512],
                                        p[:], tmps[:], ALU.mult),
                                    wait=((sac, a),), inc=sev)
                        pu_free[hp] = (sev, dv)
                        silu_prev = dv
                    w13_mm = smm.val
                    hfm_ready = sev.val
                    if s == NG - 1 and e + 1 < NE:
                        w13_ready = load_expert_w13(e + 1, ((smm, w13_mm),))
                    for u, (oc, tt) in enumerate(
                            (oc, tt) for oc in range(2) for tt in range(4)):
                        up = u % 2
                        for h in range(HC):
                            lhs = hfm[:, h * 512 + tt * 128:
                                      h * 512 + (tt + 1) * 128]
                            rhs = w2[:, h * 1024 + oc * 512:
                                     h * 1024 + (oc + 1) * 512]
                            wl = []
                            if h == 0:
                                wl = [(sev, hfm_ready), (sw, w2_ready)]
                                if po_free[up][0] is not None:
                                    wl.append(po_free[up])
                            E.emit("tensor", lambda e_, p=po[up], l=lhs,
                                   r=rhs, h=h: e_.matmul(
                                       p[:], l, r, start=(h == 0),
                                       stop=(h == HC - 1)), wait=wl,
                                   inc=(smm if h == HC - 1 else None))
                        mv = smm.val
                        gt = s * 4 + tt
                        ct = contrib[:, (u % 4) * 512:(u % 4 + 1) * 512]
                        wl = [(smm, mv)]
                        if (u % 4) in ctb_free:
                            wl.append(ctb_free[u % 4])
                        if e < 8:
                            dv = E.emit("vector", lambda e_, d=ct, p=po[up],
                                        gt=gt, ex=e: e_.tensor_scalar(
                                            d, p[:],
                                            cw[:, gt * 8 + ex: gt * 8 + ex + 1],
                                            None, ALU.mult),
                                        wait=wl + [(sev, pre_ev)], inc=sev)
                        else:
                            dv = E.emit("vector", lambda e_, d=ct, p=po[up]:
                                        e_.tensor_copy(d, p[:]),
                                        wait=wl, inc=sev)
                        po_free[up] = (sev, dv)
                        dst = acc.rearrange("(n p) d -> n p d", p=128)[gt][
                            :, oc * 512:(oc + 1) * 512]
                        E.dma("gpsimd", dst, ct,
                              wait=((sev, dv),), inc=sad,
                              accum_op=(ALU.add if e > 0 else ALU.bypass))
                        ctb_free[u % 4] = (sad, sad.val)
                    w2_mm = smm.val
                    xb_free[sl] = smm.val
                if e + 1 < NE:
                    w2_ready = load_expert_w2(e + 1, ((smm, w2_mm),))

        acc_done = sad.val
        sxp = [E.sem("sxp0"), E.sem("sxp1")]
        xg_free = 0
        apply_prev = [0, 0]
        for gt in range(NT):
            h2 = gt % 2
            asb = acc_sb[:, h2 * 1024:(h2 + 1) * 1024]
            ofr = tmpr[:, h2 * 1024:(h2 + 1) * 1024]
            xa = xg[:, (gt % 4) * 1024:(gt % 4 + 1) * 1024]
            wl = [(sad, acc_done)]
            if apply_prev[h2]:
                wl.append((sev, apply_prev[h2]))
            E.dma("sync", asb, tile(acc, gt), wait=wl, inc=sxp[h2])
            E.dma("sync", xa, tile(x_src, gt),
                  wait=(((sst, xg_free),) if gt >= 4 else ()), inc=sxp[h2])
            x_ready = sxp[h2].val
            a2, _ = emit_ln_tile(E, C, st, 4 + h2, asb,
                                 wait_x=((sxp[h2], x_ready),))
            cs = slice(4 + h2, 5 + h2)
            v = E.emit("vector", lambda e, d=ofr, s=asb, cs=cs: e.tensor_scalar(
                d, s, st["mu"][:, cs], st["rstd"][:, cs],
                ALU.subtract, ALU.mult),
                wait=((lna, a2), (lnd, E.sem("lnd").val)), inc=sev)
            apply_prev[h2] = v
            rv = E.emit("vector", lambda e, xa=xa, d=ofr: e.tensor_add(
                xa, xa, d), wait=((sev, v),), inc=sev)
            E.dma("sync", tile(x_dst, gt), xa, wait=((sev, rv),), inc=sst)
            if gt % 4 == 3:
                xg_free = sst.val



def emit_l1_attn(E, C, x_src, x_dst, W):
    NH = 8
    with ExitStack() as c:
        wqkv = E.sb(c, "t_wqkv", [128, DC * 1536], BF16)
        wout = E.sb(c, "t_wout", [128, 4 * 1024], BF16)
        lnw = E.sb(c, "t_lnw", [128, 8], F32)
        lnb = E.sb(c, "t_lnb", [128, 8], F32)
        qT = E.sb(c, "t_qT", [128, 4 * T], BF16)
        kT = E.sb(c, "t_kT", [128, 4 * T], BF16)
        vag = E.sb(c, "t_vag", [128, NT * NH * 65], BF16)
        xg = E.sb(c, "t_xg", [128, 4 * 1024], F32)
        xng = E.sb(c, "t_xng", [128, 4 * 1024], BF16)
        xnt = E.sb(c, "t_xnt", [128, DC * 512], BF16)
        qs = E.sb(c, "t_qs", [128, 512], F32)
        sqs = E.sb(c, "t_sqs", [128, 512], F32)
        ssq8 = E.sb(c, "t_ssq8", [128, 8], F32)
        rn8 = E.sb(c, "t_rn8", [128, 8], F32)
        qn = E.sb(c, "t_qn", [128, 512], BF16)
        p_t = E.sb(c, "t_pt", [128, 2 * 512], BF16)
        oag = E.sb(c, "t_oag", [128, 4 * NH * 65], BF16)
        oal = E.sb(c, "t_oal", [128, 4 * 512], BF16)
        oat = E.sb(c, "t_oat", [128, 4 * 512], BF16)
        den = E.sb(c, "t_den", [128, 8], F32)
        denr = E.sb(c, "t_denr", [128, 8], F32)
        outf = E.sb(c, "t_outf", [128, 2 * 1024], F32)
        tmpr = E.sb(c, "t_tmpr", [128, 2 * 1024], F32)
        st = mk_ln_stats(E, c, "t_st", 8)

        ptr = [E.ps(c, "t_ptr0", [128, 1024], BF16),
               E.ps(c, "t_ptr1", [128, 1024], BF16)]
        ppj = [E.ps(c, "t_ppj0", [128, 512], F32),
               E.ps(c, "t_ppj1", [128, 512], F32)]
        po = [E.ps(c, f"t_po{i}", [128, 65], F32) for i in range(4)]

        sw = E.sem("sw")
        sst = E.sem("sst")
        smm = E.sem("smm")
        sev = E.sem("sev")
        sac = E.sem("sac")
        sxp = [E.sem("sxp0"), E.sem("sxp1")]
        lna, lnd = E.sem("lna"), E.sem("lnd")

        for ci in range(DC):
            E.dma("gpsimd", wqkv[:, ci * 1536:(ci + 1) * 1536],
                  W["l1_wqkv"][ci * 128:(ci + 1) * 128, :], inc=sw)
        for ci in range(4):
            E.dma("gpsimd", wout[:, ci * 1024:(ci + 1) * 1024],
                  W["l1_wout"][ci * 128:(ci + 1) * 128, :], inc=sw)
        E.dma("gpsimd", lnw[:], W["l1_lnw"], inc=sw)
        E.dma("gpsimd", lnb[:], W["l1_lnb"], inc=sw)
        w_ready = sw.val

        vag4 = vag[:].rearrange("p (t h v) -> p t h v", t=NT, h=NH)
        ones_ev = E.emit("vector", lambda e: e.memset(
            vag4[:, :, :, 64:65], 1.0), inc=sev)

        tr_ev = [0, 0]
        pj_free = [(None, 0), (None, 0)]
        ln_gate = 0
        l2_prev = 0
        qn_used = 0
        for g in range(NG):
            h2 = g % 2
            for tt in range(4):
                E.dma("sync", xg[:, tt * 1024:(tt + 1) * 1024],
                      tile(x_src, 4 * g + tt),
                      wait=(((lnd, ln_gate),) if g > 0 else ()), inc=sxp[0])
            x_ready = sxp[0].val
            ln_done = []
            for tt in range(4):
                xa = xg[:, tt * 1024:(tt + 1) * 1024]
                na = xng[:, tt * 1024:(tt + 1) * 1024]
                _, d2 = emit_ln_tile(E, C, st, tt, xa, xn_out=na,
                                     wait_x=((sxp[0], x_ready),))
                ln_done.append(d2)
            ln_gate = E.sem("lnd").val
            for ci in range(DC):
                pt = ptr[ci % 2]
                for tt in range(4):
                    src = xng[:, tt * 1024 + ci * 128: tt * 1024 + (ci + 1) * 128]
                    dst = pt[:, tt * 128:(tt + 1) * 128]
                    wl = [(lnd, ln_done[tt])]
                    if g == 0 and ci < 2 and tt == 0:
                        wl.append(C["ready"])
                        wl.append((sw, w_ready))
                    if tr_ev[ci % 2]:
                        wl.append((sev, tr_ev[ci % 2]))
                    E.emit("tensor", lambda e, d=dst, s=src: e.transpose(
                        d, s, C["ident"][:]), wait=wl,
                        inc=(smm if tt == 3 else None))
                mv = smm.val
                tr_ev[ci % 2] = E.emit(
                    "vector", lambda e, pt=pt, ci=ci: e.tensor_scalar(
                        xnt[:, ci * 512:(ci + 1) * 512], pt[:, 0:512],
                        lnw[:, ci:ci + 1], lnb[:, ci:ci + 1],
                        ALU.mult, ALU.add),
                    wait=((smm, mv),), inc=sev)
            xnt_ready = sev.val
            for part in range(3):
                for tt in range(4):
                    gt = 4 * g + tt
                    p = ppj[tt % 2]
                    for ci in range(DC):
                        lhs = xnt[:, ci * 512 + tt * 128: ci * 512 + (tt + 1) * 128]
                        rhs = wqkv[:, ci * 1536 + part * 512:
                                   ci * 1536 + (part + 1) * 512]
                        wl = []
                        if ci == 0:
                            wl = [(sev, xnt_ready)]
                            if pj_free[tt % 2][0] is not None:
                                wl.append(pj_free[tt % 2])
                        E.emit("tensor", lambda e, p=p, l=lhs, r=rhs, ci=ci:
                               e.matmul(p[:], l, r, start=(ci == 0),
                                        stop=(ci == DC - 1)), wait=wl,
                               inc=(smm if ci == DC - 1 else None))
                    mv = smm.val
                    if part == 2:
                        dst = vag4[:, gt, :, 0:64]
                        psrc = p[:].rearrange("p (h v) -> p h v", h=8)
                        ev = E.emit("vector", lambda e, d=dst, s=psrc:
                                    e.tensor_copy(d, s),
                                    wait=((smm, mv), (sev, ones_ev)), inc=sev)
                        pj_free[tt % 2] = (sev, ev)
                    else:
                        a1 = E.emit("scalar", lambda e, p=p: e.activation(
                            sqs[:], p[:], AF.Square),
                            wait=((smm, mv), (sev, l2_prev)), inc=sac)
                        e1 = E.emit("vector", lambda e, p=p: e.tensor_copy(
                            qs[:], p[:]), wait=((sac, a1),), inc=sev)
                        pj_free[tt % 2] = (sev, e1)
                        e2 = E.emit("vector", lambda e: e.reduce_sum(
                            ssq8[:].rearrange("p (a b) -> p a b", b=1),
                            sqs[:].rearrange("p (a b) -> p a b", b=64),
                            axis=AX.X), wait=((sac, a1), (sev, e1)), inc=sev)
                        a2 = E.emit("scalar", lambda e: e.activation(
                            rn8[:], ssq8[:], AF.Ln, bias=C["l2_eps"]),
                            wait=((sev, e2),))
                        a3 = E.emit("scalar", lambda e: e.activation(
                            rn8[:], rn8[:], AF.Exp, scale=-0.5), inc=sac)
                        e3 = E.emit("vector", lambda e: e.tensor_tensor(
                            qn[:].rearrange("p (a b) -> p a b", b=64),
                            qs[:].rearrange("p (a b) -> p a b", b=64),
                            rn8[:].rearrange("p (a b) -> p a b", b=1)
                            .broadcast_to((128, 8, 64)), ALU.mult),
                            wait=((sac, a3), (smm, qn_used)), inc=sev)
                        l2_prev = e3
                        dstT = qT if part == 0 else kT
                        pt = ptr[tt % 2]
                        for cc in range(4):
                            src = qn[:, cc * 128:(cc + 1) * 128]
                            dst = pt[:, cc * 128:(cc + 1) * 128]
                            wl = [(sev, e3)]
                            if tr_ev[tt % 2]:
                                wl.append((sev, tr_ev[tt % 2]))
                            E.emit("tensor", lambda e, d=dst, s=src:
                                   e.transpose(d, s, C["ident"][:]), wait=wl,
                                   inc=(smm if cc == 3 else None))
                        mv2 = smm.val
                        qn_used = mv2
                        col = g * 512 + tt * 128
                        dst3 = dstT[:].rearrange(
                            "p (c t) -> p c t", c=4)[:, :, col:col + 128]
                        src3 = pt[:, 0:512].rearrange("p (c t) -> p c t", c=4)
                        tr_ev[tt % 2] = E.emit(
                            "vector", lambda e, d=dst3, s=src3: e.tensor_copy(
                                d, s), wait=((smm, mv2),), inc=sev)
        kv_ready = sev.val
        pre_lnd = E.sem("lnd").val
        pre_lna = E.sem("lna").val

        oag4 = oag[:].rearrange("p (t h v) -> p t h v", t=4, h=NH)
        oal4 = oal[:].rearrange("p (t d) -> p t d", t=4)
        qT4 = qT[:].rearrange("p (c t) -> p c t", c=4)
        kT4 = kT[:].rearrange("p (c t) -> p c t", c=4)
        po_free = [(None, 0)] * 4
        p_used = [0, 0]
        ppj_free = [pj_free[0], pj_free[1]]
        ln_prev = [0, 0]
        xg_free = 0
        for qg in range(NG):
            for tt in range(4):
                wl = ((sst, xg_free),) if qg > 0 else \
                    ((lnd, pre_lnd), (lna, pre_lna))
                E.dma("sync", xg[:, tt * 1024:(tt + 1) * 1024],
                      tile(x_src, 4 * qg + tt), wait=wl, inc=sxp[1])
            x_ready = sxp[1].val
            nkt = 4 * qg + 4
            for h in range(NH):
                cc = h // 2
                rb = 64 * (h % 2)
                for kt in range(nkt):
                    sl = kt % 2
                    ps = ppj[sl]
                    j = kt - 4 * qg
                    lhs = kT4[:, cc, kt * 128:(kt + 1) * 128][rb:rb + 64, :]
                    rhs = qT4[:, cc, qg * 512:(qg + 1) * 512][rb:rb + 64, :]
                    wl = [(sev, kv_ready)]
                    if ppj_free[sl][0] is not None:
                        wl.append(ppj_free[sl])
                    mv = E.emit("tensor", lambda e, p=ps, l=lhs, r=rhs:
                                e.matmul(p[:], l, r, start=True, stop=True),
                                wait=wl, inc=smm)
                    wl = [(smm, max(mv, p_used[sl]))]
                    a = E.emit("scalar", lambda e, p=ps, sl=sl: e.activation(
                        p_t[:, sl * 512:(sl + 1) * 512], p[:], AF.Exp,
                        bias=C["neg8"], scale=8.0), wait=wl, inc=sac)
                    ppj_free[sl] = (sac, a)
                    mask_ev = None
                    if j >= 0:
                        dd = p_t[:, sl * 512 + j * 128: sl * 512 + (j + 1) * 128]
                        mask_ev = E.emit("vector", lambda e, d=dd:
                                         e.tensor_tensor(d, d, C["tri"][:],
                                                         ALU.mult),
                                         wait=((sac, a),), inc=sev)
                    t0 = max(0, j)
                    for tt in range(t0, 4):
                        lhs = p_t[:, sl * 512 + tt * 128: sl * 512 + (tt + 1) * 128]
                        rhs = vag4[:, kt, h, :]
                        wl = [(sac, a)]
                        if mask_ev is not None:
                            wl.append((sev, mask_ev))
                        if kt == 0 and po_free[tt][0] is not None:
                            wl.append(po_free[tt])
                        last = (kt == 4 * qg + tt)
                        do_inc = last or (tt == 3)
                        E.emit("tensor", lambda e, p=po[tt], l=lhs, r=rhs,
                               st_=(kt == 0), sp=last: e.matmul(
                                   p[:], l, r, start=st_, stop=sp),
                               wait=wl, inc=(smm if do_inc else None))
                        if last:
                            po_mm = smm.val
                            dst = oag4[:, tt, h, :]
                            ev = E.emit("vector", lambda e, d=dst, p=po[tt]:
                                        e.tensor_copy(d, p[:]),
                                        wait=((smm, po_mm),), inc=sev)
                            po_free[tt] = (sev, ev)
                    p_used[sl] = smm.val
            div_done = []
            for tt in range(4):
                wl = [po_free[tt]]
                v = E.emit("vector", lambda e, tt=tt: e.tensor_copy(
                    den[:], oag4[:, tt, :, 64]), wait=wl, inc=sev)
                v = E.emit("vector", lambda e: e.reciprocal(denr[:], den[:]),
                           wait=((sev, v),), inc=sev)
                dv = E.emit("vector", lambda e, tt=tt: e.tensor_tensor(
                    oal4[:, tt, :].rearrange("p (h v) -> p h v", h=NH),
                    oag4[:, tt, :, 0:64],
                    denr[:].rearrange("p (h o) -> p h o", o=1).broadcast_to(
                        (128, NH, 64)), ALU.mult), wait=((sev, v),), inc=sev)
                div_done.append(dv)
            for cc2 in range(4):
                pt = ptr[cc2 % 2]
                for tt in range(4):
                    src = oal[:, tt * 512 + cc2 * 128: tt * 512 + (cc2 + 1) * 128]
                    dst = pt[:, tt * 128:(tt + 1) * 128]
                    wl = [(sev, div_done[tt])]
                    if tr_ev[cc2 % 2]:
                        wl.append((sev, tr_ev[cc2 % 2]))
                    E.emit("tensor", lambda e, d=dst, s=src: e.transpose(
                        d, s, C["ident"][:]), wait=wl,
                        inc=(smm if tt == 3 else None))
                mv = smm.val
                tr_ev[cc2 % 2] = E.emit(
                    "vector", lambda e, pt=pt, cc2=cc2: e.tensor_copy(
                        oat[:, cc2 * 512:(cc2 + 1) * 512], pt[:, 0:512]),
                    wait=((smm, mv),), inc=sev)
            oat_ready = sev.val
            for tt in range(4):
                h2 = tt % 2
                of = outf[:, h2 * 1024:(h2 + 1) * 1024]
                ofr = tmpr[:, h2 * 1024:(h2 + 1) * 1024]
                ev_out = []
                for oc in range(2):
                    p = ppj[oc % 2]
                    for cc2 in range(4):
                        lhs = oat[:, cc2 * 512 + tt * 128: cc2 * 512 + (tt + 1) * 128]
                        rhs = wout[:, cc2 * 1024 + oc * 512:
                                   cc2 * 1024 + (oc + 1) * 512]
                        wl = []
                        if cc2 == 0:
                            wl = [(sev, oat_ready)]
                            if ppj_free[oc % 2][0] is not None:
                                wl.append(ppj_free[oc % 2])
                        E.emit("tensor", lambda e, p=p, l=lhs, r=rhs, ci=cc2:
                               e.matmul(p[:], l, r, start=(ci == 0),
                                        stop=(ci == 3)), wait=wl,
                               inc=(smm if cc2 == 3 else None))
                    mv = smm.val
                    wlv = [(smm, mv)]
                    if ln_prev[h2]:
                        wlv.append((lna, ln_prev[h2]))
                    ev = E.emit("vector", lambda e, d=of, oc=oc, p=p:
                                e.tensor_copy(d[:, oc * 512:(oc + 1) * 512],
                                              p[:]), wait=wlv, inc=sev)
                    ev_out.append(ev)
                    ppj_free[oc % 2] = (sev, ev)
                a2, _ = emit_ln_tile(E, C, st, 4 + h2, of,
                                     wait_x=((sev, ev_out[-1]),))
                ln_prev[h2] = a2
                cs = slice(4 + h2, 5 + h2)
                xa = xg[:, tt * 1024:(tt + 1) * 1024]
                v = E.emit("vector", lambda e, d=ofr, s=of, cs=cs:
                           e.tensor_scalar(d, s, st["mu"][:, cs],
                                           st["rstd"][:, cs],
                                           ALU.subtract, ALU.mult),
                           wait=((lna, a2), (lnd, E.sem("lnd").val)), inc=sev)
                import os as _os
                if _os.environ.get("DBG_L1OUT"):
                    rv = E.emit("vector", lambda e, xa=xa, tt=tt:
                                e.tensor_copy(xa[:, 0:512],
                                              oal[:, tt * 512:(tt + 1) * 512]),
                                wait=((sev, v), (sxp[1], x_ready)), inc=sev)
                else:
                    rv = E.emit("vector", lambda e, xa=xa, d=ofr:
                                e.tensor_add(xa, xa, d),
                                wait=((sev, v), (sxp[1], x_ready)), inc=sev)
                E.dma("sync", tile(x_dst, 4 * qg + tt), xa,
                      wait=((sev, rv),), inc=sst)
            xg_free = sst.val



def emit_l1_ffn(E, C, x_src, x_dst, W):
    HC3 = 24
    NG2 = 4
    with ExitStack() as c:
        wd = E.sb(c, "f_wd", [128, HC3 * 1024], BF16)
        wg_b = [E.sb(c, "f_wg0", [128, DC * 128], BF16),
                E.sb(c, "f_wg1", [128, DC * 128], BF16)]
        wu_b = [E.sb(c, "f_wu0", [128, DC * 128], BF16),
                E.sb(c, "f_wu1", [128, DC * 128], BF16)]
        xnt = E.sb(c, "f_xnt", [128, DC * 1024], BF16)
        hfm = E.sb(c, "f_hfm", [128, HC3 * 1024], BF16)
        xg = E.sb(c, "f_xg", [128, 4 * 1024], F32)
        xng = E.sb(c, "f_xng", [128, 4 * 1024], BF16)
        tmps = E.sb(c, "f_tmps", [128, 512], F32)
        outf = E.sb(c, "f_outf", [128, 2 * 1024], F32)
        tmpr = E.sb(c, "f_tmpr", [128, 2 * 1024], F32)
        st = mk_ln_stats(E, c, "f_st", 8)

        ptr = [E.ps(c, "f_ptr0", [128, 1024], BF16),
               E.ps(c, "f_ptr1", [128, 1024], BF16)]
        pg = [E.ps(c, "f_pg0", [128, 512], F32),
              E.ps(c, "f_pg1", [128, 512], F32)]
        pu = [E.ps(c, "f_pu0", [128, 512], F32),
              E.ps(c, "f_pu1", [128, 512], F32)]
        pd = [E.ps(c, "f_pd0", [128, 512], F32),
              E.ps(c, "f_pd1", [128, 512], F32)]

        sw = E.sem("sw")
        swg = E.sem("swg")
        sst = E.sem("sst")
        smm = E.sem("smm")
        sev = E.sem("sev")
        sac = E.sem("sac")
        sxp = [E.sem("sxp0"), E.sem("sxp1")]
        lna, lnd = E.sem("lna"), E.sem("lnd")

        for hc in range(HC3):
            E.dma("gpsimd", wd[:, hc * 1024:(hc + 1) * 1024],
                  W["l1_wdt"][hc * 128:(hc + 1) * 128, :], inc=sw)
        wd_ready = sw.val

        def load_wgu(g2, h, slot, wait):
            E.dma("gpsimd",
                  wg_b[slot][:].rearrange("p (c k) -> p c k", c=DC),
                  W["l1_wgt"][h].rearrange("(c p) k -> p c k", p=128),
                  wait=wait, inc=swg)
            E.dma("gpsimd",
                  wu_b[slot][:].rearrange("p (c k) -> p c k", c=DC),
                  W["l1_wut"][h].rearrange("(c p) k -> p c k", p=128),
                  wait=wait, inc=swg)
            return swg.val

        tr_ev = [0, 0]
        pgu_free = [(None, 0), (None, 0)]
        pd_free = [(None, 0), (None, 0)]
        wgu_used = [0, 0]
        silu_prev = 0
        ln_prev = [0, 0]
        ln_gate = 0
        xg_free = 0
        st_gate = 0

        for g2 in range(NG2):
            ln_apply = []
            for sub in range(2):
                for tt in range(4):
                    gt = 8 * g2 + 4 * sub + tt
                    wl = []
                    if gt >= 4:
                        wl.append((lnd, ln_gate))
                    if g2 > 0:
                        wl.append((sst, xg_free))
                    E.dma("sync", xg[:, tt * 1024:(tt + 1) * 1024],
                          tile(x_src, gt), wait=wl, inc=sxp[0])
                x_ready = sxp[0].val
                for tt in range(4):
                    xa = xg[:, tt * 1024:(tt + 1) * 1024]
                    na = xng[:, tt * 1024:(tt + 1) * 1024]
                    _, d2 = emit_ln_tile(E, C, st, tt, xa, xn_out=na,
                                         wait_x=((sxp[0], x_ready),))
                    ln_apply.append(d2)
                ln_gate = E.sem("lnd").val
                for ci in range(DC):
                    pt = ptr[ci % 2]
                    for tt in range(4):
                        src = xng[:, tt * 1024 + ci * 128:
                                  tt * 1024 + (ci + 1) * 128]
                        dst = pt[:, tt * 128:(tt + 1) * 128]
                        wl = [(lnd, ln_apply[sub * 4 + tt])]
                        if g2 == 0 and sub == 0 and ci < 2 and tt == 0:
                            wl.append(C["ready"])
                        if tr_ev[ci % 2]:
                            wl.append((sev, tr_ev[ci % 2]))
                        E.emit("tensor", lambda e, d=dst, s=src: e.transpose(
                            d, s, C["ident"][:]), wait=wl,
                            inc=(smm if tt == 3 else None))
                    mv = smm.val
                    dst = xnt[:, ci * 1024 + sub * 512: ci * 1024 + (sub + 1) * 512]
                    tr_ev[ci % 2] = E.emit(
                        "vector", lambda e, d=dst, pt=pt: e.tensor_copy(
                            d, pt[:, 0:512]), wait=((smm, mv),), inc=sev)
            xnt_ready = sev.val

            w_rdy = load_wgu(g2, 0, 0, ((smm, wgu_used[0]),))
            for h in range(HC3):
                hp = h % 2
                if h + 1 < HC3:
                    nx = load_wgu(g2, h + 1, (h + 1) % 2,
                                  ((smm, wgu_used[(h + 1) % 2]),))
                for sub in range(2):
                    for which, wb, pp, pfree in (
                            ("g", wg_b[hp], pg[sub], pgu_free[sub]),
                            ("u", wu_b[hp], pu[sub], pgu_free[sub])):
                        for ci in range(DC):
                            lhs = wb[:, ci * 128:(ci + 1) * 128]
                            rhs = xnt[:, ci * 1024 + sub * 512:
                                      ci * 1024 + (sub + 1) * 512]
                            wl = []
                            if ci == 0:
                                wl = [(sev, xnt_ready), (swg, w_rdy)]
                                if which == "g" and pfree[0] is not None:
                                    wl.append(pfree)
                            E.emit("tensor", lambda e, p=pp, l=lhs, r=rhs,
                                   ci=ci: e.matmul(p[:], l, r,
                                                   start=(ci == 0),
                                                   stop=(ci == DC - 1)),
                                   wait=wl,
                                   inc=(smm if ci == DC - 1 else None))
                    mv = smm.val
                    wgu_used[hp] = mv
                    a = E.emit("scalar", lambda e, p=pg[sub]: e.activation(
                        tmps[:], p[:], AF.Silu, scale=C["dq"]),
                        wait=((smm, mv), (sev, silu_prev)), inc=sac)
                    dv = E.emit("vector", lambda e, p=pu[sub], h=h, sub=sub:
                                e.scalar_tensor_tensor(
                                    hfm[:, h * 1024 + sub * 512:
                                        h * 1024 + (sub + 1) * 512],
                                    p[:], 1.0 / 127.0, tmps[:],
                                    ALU.mult, ALU.mult),
                                wait=((sac, a),), inc=sev)
                    pgu_free[sub] = (sev, dv)
                    silu_prev = dv
                w_rdy = nx if h + 1 < HC3 else w_rdy
            hfm_ready = sev.val

            for sub in range(2):
                if sub == 0:
                    wl0 = [(lnd, ln_gate), (lna, E.sem("lna").val)]
                    if g2 > 0 or True:
                        wl0.append((sst, xg_free))
                else:
                    wl0 = [(sst, xg_free)]
                for tt in range(4):
                    gt = 8 * g2 + 4 * sub + tt
                    E.dma("sync", xg[:, tt * 1024:(tt + 1) * 1024],
                          tile(x_src, gt),
                          wait=(wl0 if tt == 0 else ()), inc=sxp[1])
                for tt in range(4):
                    gt = 8 * g2 + 4 * sub + tt
                    h2 = tt % 2
                    of = outf[:, h2 * 1024:(h2 + 1) * 1024]
                    ofr = tmpr[:, h2 * 1024:(h2 + 1) * 1024]
                    ev_out = []
                    for oc in range(2):
                        p = pd[oc % 2]
                        for h in range(HC3):
                            lhs = hfm[:, h * 1024 + sub * 512 + tt * 128:
                                      h * 1024 + sub * 512 + (tt + 1) * 128]
                            rhs = wd[:, h * 1024 + oc * 512:
                                     h * 1024 + (oc + 1) * 512]
                            wl = []
                            if h == 0:
                                wl = [(sev, hfm_ready), (sw, wd_ready)]
                                if pd_free[oc % 2][0] is not None:
                                    wl.append(pd_free[oc % 2])
                            E.emit("tensor", lambda e, p=p, l=lhs, r=rhs,
                                   h=h: e.matmul(p[:], l, r,
                                                 start=(h == 0),
                                                 stop=(h == HC3 - 1)),
                                   wait=wl,
                                   inc=(smm if h == HC3 - 1 else None))
                        mv = smm.val
                        wlv = [(smm, mv)]
                        if ln_prev[h2]:
                            wlv.append((lna, ln_prev[h2]))
                        ev = E.emit("scalar", lambda e, d=of, oc=oc, p=p:
                                    e.activation(
                                        d[:, oc * 512:(oc + 1) * 512], p[:],
                                        AF.Copy, scale=C["dq"]),
                                    wait=wlv, inc=sac)
                        ev_out.append(ev)
                        pd_free[oc % 2] = (sac, ev)
                    a2, _ = emit_ln_tile(E, C, st, 4 + h2, of,
                                         wait_x=((sac, ev_out[-1]),))
                    ln_prev[h2] = a2
                    cs = slice(4 + h2, 5 + h2)
                    xa = xg[:, tt * 1024:(tt + 1) * 1024]
                    v = E.emit("vector", lambda e, d=ofr, s=of, cs=cs:
                               e.tensor_scalar(d, s, st["mu"][:, cs],
                                               st["rstd"][:, cs],
                                               ALU.subtract, ALU.mult),
                               wait=((lna, a2), (lnd, E.sem("lnd").val)),
                               inc=sev)
                    rv = E.emit("vector", lambda e, xa=xa, d=ofr:
                                e.tensor_add(xa, xa, d),
                                wait=((sev, v), (sxp[1], sxp[1].val)),
                                inc=sev)
                    E.dma("sync", tile(x_dst, gt), xa,
                          wait=((sev, rv),), inc=sst)
                xg_free = sst.val



BLOCKS_ALL = ("l0_attn", "l0_moe", "l1_attn", "l1_ffn")


def build_program(blocks=BLOCKS_ALL):
    nc = bass.Bass("TRN2", target_bir_lowering=False)
    dram = {}
    dram["x_in"] = nc.dram_tensor("x_in", [T, D], F32, kind="ExternalInput").ap()
    dram["l0_wq"] = nc.dram_tensor("l0_wq", [D, D], F32, kind="ExternalInput").ap()
    dram["l0_wk"] = nc.dram_tensor("l0_wk", [D, D], F32, kind="ExternalInput").ap()
    dram["l0_wv"] = nc.dram_tensor("l0_wv", [D, D], F32, kind="ExternalInput").ap()
    dram["l0_wo"] = nc.dram_tensor("l0_wo", [D, D], F32, kind="ExternalInput").ap()
    dram["moe_w1t"] = nc.dram_tensor("moe_w1t", [10, D, 2048], F32, kind="ExternalInput").ap()
    dram["moe_w3t"] = nc.dram_tensor("moe_w3t", [10, D, 2048], F32, kind="ExternalInput").ap()
    dram["moe_w2t"] = nc.dram_tensor("moe_w2t", [10, 2048, D], F32, kind="ExternalInput").ap()
    dram["router_t"] = nc.dram_tensor("router_t", [D, 8], F32, kind="ExternalInput").ap()
    dram["router_bias"] = nc.dram_tensor("router_bias", [1, 8], F32, kind="ExternalInput").ap()
    dram["l1_wqkv"] = nc.dram_tensor("l1_wqkv", [D, 1536], F32, kind="ExternalInput").ap()
    dram["l1_wout"] = nc.dram_tensor("l1_wout", [512, D], F32, kind="ExternalInput").ap()
    dram["l1_lnw"] = nc.dram_tensor("l1_lnw", [128, 8], F32, kind="ExternalInput").ap()
    dram["l1_lnb"] = nc.dram_tensor("l1_lnb", [128, 8], F32, kind="ExternalInput").ap()
    dram["l1_wgt"] = nc.dram_tensor("l1_wgt", [24, D, 128], I32, kind="ExternalInput").ap()
    dram["l1_wut"] = nc.dram_tensor("l1_wut", [24, D, 128], I32, kind="ExternalInput").ap()
    dram["l1_wdt"] = nc.dram_tensor("l1_wdt", [3072, D], I32, kind="ExternalInput").ap()
    dram["x_out"] = nc.dram_tensor("x_out", [T, D], F32, kind="ExternalOutput").ap()
    dram["xa"] = nc.dram_tensor("xa", [T, D], F32, kind="Internal").ap()
    dram["xb"] = nc.dram_tensor("xb", [T, D], F32, kind="Internal").ap()
    dram["moe_acc"] = nc.dram_tensor("moe_acc", [T, D], F32, kind="Internal").ap()
    dram["xnt_spill"] = nc.dram_tensor("xnt_spill", [D, T], BF16, kind="Internal").ap()

    with ExitStack() as ctx:
        E = Env(nc, ctx)
        C = emit_consts(E, ctx)

        chain = [dram["x_in"]]
        for i in range(len(blocks) - 1):
            chain.append(dram["xa"] if i % 2 == 0 else dram["xb"])
        chain.append(dram["x_out"])

        for i, b in enumerate(blocks):
            src, dst = chain[i], chain[i + 1]
            if b == "l0_attn":
                emit_l0_attn(E, C, src, dst, dram)
            elif b == "l0_moe":
                emit_l0_moe(E, C, src, dst, dram)
            elif b == "l1_attn":
                emit_l1_attn(E, C, src, dst, dram)
            elif b == "l1_ffn":
                emit_l1_ffn(E, C, src, dst, dram)
            else:
                raise ValueError(b)
            E.barrier()

        sst = E.sem("sst")
        E.emit("sync", lambda e: e.nop(), wait=((sst, sst.val),))
        assert E.check_deadlock(), "semaphore deadlock in emitted program"
        E.replay()
    return nc



def prep_feeds(inp):
    cnt = np.ascontiguousarray
    f = {}
    f["l0_wq"] = cnt(np.asarray(inp["L0_attn_Wq"]).T)
    f["l0_wk"] = cnt(np.asarray(inp["L0_attn_Wk"]).T)
    f["l0_wv"] = cnt(np.asarray(inp["L0_attn_Wv"]).T)
    f["l0_wo"] = cnt(np.asarray(inp["L0_attn_Wo"]).T)
    w1 = np.concatenate([np.asarray(inp["L0_ffn_w1"]),
                         np.asarray(inp["L0_ffn_sw1"])], axis=0)
    w3 = np.concatenate([np.asarray(inp["L0_ffn_w3"]),
                         np.asarray(inp["L0_ffn_sw3"])], axis=0)
    w2 = np.concatenate([np.asarray(inp["L0_ffn_w2"]),
                         np.asarray(inp["L0_ffn_sw2"])], axis=0)
    f["moe_w1t"] = cnt(w1.transpose(0, 2, 1))
    f["moe_w3t"] = cnt(w3.transpose(0, 2, 1))
    f["moe_w2t"] = cnt(w2.transpose(0, 2, 1))
    f["router_t"] = cnt(np.asarray(inp["L0_ffn_router_w"]).T)
    f["router_bias"] = cnt(np.asarray(inp["L0_ffn_expert_bias"]).reshape(1, 8))
    f["l1_wqkv"] = cnt(np.asarray(inp["L1_attn_w_qkv"]))
    f["l1_wout"] = cnt(np.asarray(inp["L1_attn_w_out"]))
    f["l1_lnw"] = cnt(np.asarray(inp["L1_attn_ln_w"]).reshape(8, 128).T)
    f["l1_lnb"] = cnt(np.asarray(inp["L1_attn_ln_b"]).reshape(8, 128).T)
    wgt = np.asarray(inp["L1_ffn_w_gate"]).T
    wut = np.asarray(inp["L1_ffn_w_up"]).T
    f["l1_wgt"] = cnt(wgt.reshape(1024, 24, 128).transpose(1, 0, 2))
    f["l1_wut"] = cnt(wut.reshape(1024, 24, 128).transpose(1, 0, 2))
    f["l1_wdt"] = cnt(np.asarray(inp["L1_ffn_w_down"]).T)
    return f


def kernel(**inp):
    x = np.asarray(inp["L0_attn_x"])
    feeds = prep_feeds(inp)
    nc = build_program()
    in_maps = []
    for i in range(8):
        m = dict(feeds)
        m["x_in"] = np.ascontiguousarray(x[i])
        in_maps.append(m)
    res = run_bass_kernel_spmd(nc, in_maps, core_ids=list(range(8)))
    return np.stack([r["x_out"] for r in res.results], axis=0)
